# version 04 (final)
# baseline (speedup 1.0000x reference)
import numpy as np
from contextlib import ExitStack
import concourse.bass as bass
import concourse.mybir as mybir
from concourse.bass_utils import run_bass_kernel_spmd

F32 = mybir.dt.float32
AF = mybir.ActivationFunctionType
ALU = mybir.AluOpType
AXX = mybir.AxisListType.X

BATCH, SEQ, DM = 4, 2048, 4096
NCORE = 8
LT = 2048
OT = 1024
TB = 512
NCH = DM // 128
SCALE_A = float(1.0 / np.sqrt(128.0))
KEXP, RANK = 64, 16
TOPK = 16
FFN_SCALING = 2.0
NHB, HD = 128, 64
DIN = NHB * HD
G = 8
NST = 128
KCONV = 4
CS = 256
NCK = LT // CS
EPS = 1e-5
CONV_DIM = DIN + 2 * G * NST
PROJ = DIN + CONV_DIM + NHB
NFT_IN = PROJ // 128
NFT_XBC = CONV_DIM // 128
NFT_GATE = DIN // 128
BIG = 1.0e30

ENGINES = ("sync", "vector", "scalar", "tensor", "gpsimd")
DBG = [False]


_EPOCHS = [
    ["wqred", "ldwq", "s1mm", "s1ev", "s1tr", "s1cc", "s1ct", "s1ce",
     "s1qw", "s1qb", "s1st", "s1ex", "s1nz", "s1pz", "s1oq", "s1oo",
     "s1od", "s1rx", "s1ra", "s1rw"],
    ["ldwr", "f2sc", "f2se", "f2tt", "f2gs", "f2ge", "f2gd", "f2gt",
     "f2gc", "f2gw", "f2gb", "f2aw", "f2hp", "f2hm", "f2bw", "f2op",
     "f2ov", "f2om"],
    ["p3wl", "p3mm", "p3ev", "p3sw", "c3ld", "c3xl", "c3f", "c3a", "c3b",
     "c3c", "c3d", "c3e", "c3g", "c3s", "c3w"],
    ["s3z", "d3s1", "d3s2", "d3s3", "d3s4", "d3s5",
     "d3ld", "d3a", "d3b", "d3c", "d3d", "d3e", "d3f", "d3tr",
     "d3te", "d3bc", "d3bt", "d3be", "d3xl", "d3xt", "d3xe", "d3sd",
     "y3gm", "y3ge", "y3rw", "y3rr", "y3t1", "y3t2", "y3t3", "y3dj",
     "y3ob", "y3ev", "y3yt", "y3ye", "y3yw", "t3sm", "t3su", "t3fl",
     "t3sg"],
    ["n3ld", "n3sl", "n3ym", "n3mj", "n3sq", "n3vr", "n3rs", "n3rw",
     "n3rb", "n3gy", "n3r2", "n3rn", "n3wb"],
    ["o3yl", "o3wl", "o3mm", "o3ev", "o3rl", "o3pw"],
]
_ALIAS = {}
for _ep in _EPOCHS:
    for _i, _n in enumerate(_ep):
        _ALIAS[_n] = f"ep{_i}"


class _CntView:
    def __init__(self):
        self.d = {}

    def _k(self, name):
        return _ALIAS.get(name, name)

    def __getitem__(self, name):
        return self.d[self._k(name)]

    def __setitem__(self, name, v):
        self.d[self._k(name)] = v

    def __contains__(self, name):
        return self._k(name) in self.d

    def get(self, name, dflt=0):
        return self.d.get(self._k(name), dflt)


class Ctx:

    def __init__(self, nc, es):
        self.nc = nc
        self.es = es
        self.q = {e: [] for e in ENGINES}
        self.sem_h = {}
        self.cnt = _CntView()

    def sem(self, name):
        name = _ALIAS.get(name, name)
        if name not in self.sem_h:
            self.sem_h[name] = self.es.enter_context(self.nc.semaphore(name))
            self.cnt[name] = 0
        return self.sem_h[name]

    def op(self, eng, fname, *args, inc=None, incn=1, **kwargs):
        h = None
        if inc is not None:
            h = self.sem(inc)
            self.cnt[inc] += incn

        def f(E, fname=fname, args=args, kwargs=kwargs, h=h, incn=incn):
            r = getattr(E, fname)(*args, **kwargs)
            if h is not None:
                r.then_inc(h, incn)
        self.q[eng].append(f)
        return self.cnt[inc] if inc is not None else None

    def wait(self, eng, sem, val=None):
        h = self.sem(sem)
        if val is None:
            val = self.cnt[sem]

        def f(E, h=h, val=val):
            E.wait_ge(h, val)
        self.q[eng].append(f)

    def dma(self, eng, out, in_, sem):
        return self.op(eng, "dma_start", inc=sem, incn=16, out=out, in_=in_)

    def barrier(self, tag):
        s = "bar"
        self.op("sync", "nop", inc=s)
        self.op("tensor", "nop", inc=s)
        self.op("vector", "memset", self.dumv, 0.0, inc=s)
        self.op("scalar", "memzero", self.dums, inc=s)
        self.op("gpsimd", "memset", self.dumg, 0.0, inc=s)
        v = self.cnt[s]
        for e in ENGINES:
            self.wait(e, s, v)

    def fence(self, eng, tag=None):
        s = "fn_" + eng
        d = {"vector": self.dumv, "gpsimd": self.dumg}.get(eng)
        if eng == "scalar":
            self.op(eng, "memzero", self.dums, inc=s)
        else:
            self.op(eng, "memset", d, 0.0, inc=s)
        self.wait(eng, s)

    def finalize(self, block):
        for e, fn in (("sync", block.sync), ("vector", block.vector),
                      ("scalar", block.scalar), ("tensor", block.tensor),
                      ("gpsimd", block.gpsimd)):
            ops = self.q[e]

            def body(E, ops=ops):
                for f in ops:
                    f(E)
            fn(body)


class Arena:
    def __init__(self, ap, globals_end=0):
        self.ap = ap
        self.t = ap.tensor
        self.W = ap.tensor.shape[1]
        self.g = globals_end
        self.off = globals_end

    def glob(self, cols):
        a = self.ap[:, self.g:self.g + cols]
        self.g += cols
        assert self.g <= self.W
        self.off = max(self.off, self.g)
        return a

    def reset(self):
        self.off = self.g

    def take(self, cols):
        a = self.ap[:, self.off:self.off + cols]
        self.off += cols
        assert self.off <= self.W, f"arena overflow {self.off} > {self.W}"
        return a


def build_program(taps=(), stop_after=None):
    nc = bass.Bass("TRN2", target_bir_lowering=False)
    es = ExitStack()
    C = Ctx(nc, es)

    def dram(name, shape):
        k = "ExternalOutput" if name in taps else "Internal"
        return nc.dram_tensor(name, list(shape), F32, kind=k).ap()

    def din(name, shape):
        return nc.dram_tensor(name, list(shape), F32, kind="ExternalInput").ap()

    xin = din("xin", (NCH, 128, LT))
    wq_raw = din("wq_raw", (NCH, 128, 4096))
    wk_t = din("wk_t", (NCH, 128, 128))
    wv_t = din("wv_t", (NCH, 128, 128))
    bq_in = din("bq_in", (128, 32))
    bk_in = din("bk_in", (128, 1))
    bv_in = din("bv_in", (128, 1))
    wr_t = din("wr_t", (NCH, 128, KEXP))
    a_t = din("a_t", (8, 128, NCH * 128))
    b_t = din("b_t", (NCH, 128, 8 * 128))
    w1_t = din("w1_t", (NFT_IN, 128, NCH * 128))
    w2_t = din("w2_t", (NCH, 128, NFT_GATE * 128))
    convw = din("convw", (NFT_XBC, 128, KCONV))
    convb = din("convb", (NFT_XBC, 128, 1))
    dtbias = din("dtbias", (128, 1))
    alog = din("alog", (128, 1))
    dexp = din("dexp", (NFT_GATE, 128, 1))
    normw = din("normw", (NFT_GATE, 128, 1))
    ident_in = din("ident_in", (128, 128))
    flag_in = din("flag_in", (128, 1))
    tril_in = din("tril_in", (128, 128))

    yout = nc.dram_tensor("yout", [NCH, 128, OT], F32, kind="ExternalOutput").ap()

    x1s = dram("x1s", (NCH, 128, LT))
    x2s = dram("x2s", (NCH, 128, LT))
    x3s = dram("x3s", (NCH, 128, OT))
    o_scr = dram("o_scr", (128, LT))
    qst_s = dram("qst_s", (TB, 128))
    crow_s = dram("crow_s", (1, TB))
    rrow_s = dram("rrow_s", (8, TB))
    gf_s = dram("gf_s", (KEXP, TB))
    gate_s = dram("gate_s", (NFT_GATE, 128, OT))
    xbc0_s = dram("xbc0_s", (NFT_XBC, 128, LT))
    xbcs_s = dram("xbcs_s", (NFT_XBC, 128, LT))
    dt_s = dram("dt_s", (128, LT))
    yfm_s = dram("yfm_s", (NFT_GATE, 128, OT))
    ygn_s = dram("ygn_s", (NFT_GATE, 128, OT))
    rr_s = dram("rr_s", (16 * 128,))
    dbg_s = dram("dbg_s", (10, 128, 512))
    srun_s = dram("srun_s", (NCK, 128, DIN))
    dbg2 = dram("dbg2", (3, 128, DIN))
    er_s = dram("er_s", (128,))

    AR_W = 46080
    arena_h = es.enter_context(nc.sbuf_tensor("arena", [128, AR_W], F32))
    AR = Arena(arena_h.ap())
    psb = [es.enter_context(nc.psum_tensor(f"psb{i}", [128, 512], F32)).ap()
           for i in range(8)]

    identt = AR.glob(128)
    ones_c = AR.glob(1)
    colbuf = AR.glob(40)
    dtcol = AR.glob(8)
    cwcol = AR.glob(5 * NFT_XBC)
    nwcol = AR.glob(NFT_GATE)
    dxcol = AR.glob(NFT_GATE)
    trilm = AR.glob(128)
    rowbuf = AR.glob(2 * TB)
    epscol = AR.glob(1)
    flagc = AR.glob(1)
    C.dumv = AR.glob(1)[:, :]
    C.dums = AR.glob(1)[:, :]
    C.dumg = AR.glob(1)[:, :]

    def tdrain(incname, incn=1):
        C.op("tensor", "transpose", psb[6][:, 384:512], identt[:, :],
             identt[:, :], inc=incname, incn=incn)

    LD = "ld0"
    C.dma("sync", identt[:, :], ident_in[:, :], LD)
    C.dma("sync", trilm[:, :], tril_in[:, :], LD)
    C.dma("sync", colbuf[:, 0:1], bk_in[:, :], LD)
    C.dma("sync", colbuf[:, 1:2], bv_in[:, :], LD)
    C.dma("sync", colbuf[:, 2:34], bq_in[:, :], LD)
    C.dma("sync", dtcol[:, 0:1], dtbias[:, :], LD)
    C.dma("sync", dtcol[:, 1:2], alog[:, :], LD)
    C.dma("sync", flagc[:, :], flag_in[:, :], LD)
    for ft in range(NFT_XBC):
        C.dma("sync", cwcol[:, 5 * ft:5 * ft + 4], convw[ft], LD)
        C.dma("sync", cwcol[:, 5 * ft + 4:5 * ft + 5], convb[ft], LD)
    for ft in range(NFT_GATE):
        C.dma("sync", nwcol[:, ft:ft + 1], normw[ft], LD)
        C.dma("sync", dxcol[:, ft:ft + 1], dexp[ft], LD)
    C.wait("vector", LD)
    C.wait("scalar", LD)
    C.op("vector", "memset", ones_c[:, :], 1.0)
    C.op("vector", "memset", epscol[:, :], EPS)
    C.op("scalar", "activation", out=colbuf[:, 34:35], in_=dtcol[:, 1:2],
         func=AF.Exp, inc="ini_s")
    C.wait("vector", "ini_s")
    C.op("vector", "tensor_scalar_mul", dtcol[:, 3:4], colbuf[:, 34:35], -1.0)
    C.op("vector", "tensor_reduce", colbuf[:, 35:36], colbuf[:, 2:34], AXX,
         ALU.add, inc="ini_v")
    C.barrier("init")

    def emit_ln(src_t, src_tt, t0, dsts, sq, bc):
        s_ld = "lnld"
        for c in range(NCH):
            C.dma("gpsimd", dsts[c],
                  bass.AP(src_t.tensor, c * 128 * src_tt + t0,
                          [[src_tt, 128], [1, TB]]), s_ld)
        C.wait("scalar", s_ld)
        C.wait("tensor", s_ld)
        C.wait("tensor", "lnvr", C.cnt.get("lnvr", 0))
        for c in range(NCH):
            sqb = sq[c % 2]
            if c >= 2:
                C.wait("scalar", "lnmu", self_cnt_lnmu + c - 1)
            C.op("scalar", "activation", out=sqb, in_=dsts[c],
                 func=AF.Square, inc="lnsq")
            C.wait("tensor", "lnsq", C.cnt["lnsq"])
            C.op("tensor", "matmul", psb[6][0:1, 0:TB], ones_c[:, :],
                 dsts[c], start=(c == 0), stop=(c == NCH - 1))
            C.op("tensor", "matmul", psb[7][0:1, 0:TB], ones_c[:, :], sqb,
                 start=(c == 0), stop=(c == NCH - 1), inc="lnmu")
        C.wait("vector", "lnmu")
        rowA = rowbuf[0:1, 0:TB]
        rowB = rowbuf[0:1, TB:2 * TB]
        C.op("vector", "tensor_scalar_mul", rowA, psb[6][0:1, 0:TB], 1.0 / DM)
        C.op("vector", "tensor_scalar_mul", rowB, psb[7][0:1, 0:TB], 1.0 / DM)
        C.op("vector", "tensor_tensor", out=sq[0][0:1, 0:TB], in0=rowA,
             in1=rowA, op=ALU.mult)
        C.op("vector", "tensor_tensor", out=rowB, in0=rowB,
             in1=sq[0][0:1, 0:TB], op=ALU.subtract, inc="lnvr")
        C.wait("scalar", "lnvr")
        C.op("scalar", "activation", out=rowB, in_=rowB,
             func=AF.Sqrt, bias=epscol[0:1, 0:1], inc="lnsd")
        C.wait("vector", "lnsd")
        C.op("vector", "reciprocal", rowB, rowB)
        C.op("vector", "tensor_tensor", out=rowA, in0=rowA,
             in1=rowB, op=ALU.mult, inc="lnrw")
        C.wait("gpsimd", "lnrw")
        C.dma("gpsimd", rrow_s[0, :], rowB, s_ld)
        C.dma("gpsimd", rrow_s[1, :], rowA, s_ld)
        C.dma("gpsimd", bc[0], rrow_s[0:1, :].to_broadcast([128, TB]), s_ld)
        C.dma("gpsimd", bc[1], rrow_s[1:2, :].to_broadcast([128, TB]), s_ld)
        C.wait("vector", s_ld)
        for c in range(NCH):
            C.op("vector", "tensor_tensor", out=dsts[c], in0=dsts[c],
                 in1=bc[0], op=ALU.mult)
            C.op("vector", "tensor_tensor", out=dsts[c], in0=dsts[c],
                 in1=bc[1], op=ALU.subtract,
                 inc=("lnap" if c == NCH - 1 else None))
        v = C.cnt["lnap"]
        for e in ("tensor", "scalar", "gpsimd"):
            C.wait(e, "lnap", v)

    AR.reset()
    lnb = AR.take(NCH * TB)
    qkvw = AR.take(NCH * 3 * 128)
    med_a = AR.take(2048)
    med_b = AR.take(32)
    med_c = AR.take(2 * TB)
    med_d = AR.take(2048)
    sm = [AR.take(512) for _ in range(6)]
    sq2 = AR.take(512)
    ra_ = [AR.take(512) for _ in range(2)]
    rb_ = [AR.take(512) for _ in range(2)]
    lnds = [lnb[:, c * TB:(c + 1) * TB] for c in range(NCH)]
    lnsq = (sm[0], sq2)
    lnbc = (sm[1], sm[2])
    global self_cnt_lnmu
    self_cnt_lnmu = 0

    for c in range(NCH):
        r = c % 4
        if c >= 4:
            C.wait("sync", "wqred", c - 3)
        C.dma("sync", lnb[:, r * 4096:(r + 1) * 4096], wq_raw[c], "ldwq")
        C.wait("vector", "ldwq", 16 * (c + 1))
        C.op("vector", "tensor_reduce", qkvw[:, c * 384:c * 384 + 128],
             lnb[:, r * 4096:(r + 1) * 4096].rearrange("p (d h) -> p d h", h=32),
             AXX, ALU.add, inc="wqred")
    for c in range(NCH):
        C.dma("sync", qkvw[:, c * 384 + 128:c * 384 + 256], wk_t[c], "ldwq")
        C.dma("sync", qkvw[:, c * 384 + 256:c * 384 + 384], wv_t[c], "ldwq")
    C.wait("sync", "ldwq")
    C.wait("vector", "wqred", NCH)
    C.barrier("qkvw")

    def stage1_block(blk):
        t0 = blk * TB
        global self_cnt_lnmu
        self_cnt_lnmu = C.cnt.get("lnmu", 0)
        emit_ln(xin, LT, t0, lnds, lnsq, lnbc)
        for c in range(NCH):
            w0 = c * 384
            C.op("tensor", "matmul", psb[0][:, 0:TB], qkvw[:, w0:w0 + 128],
                 lnds[c], start=(c == 0), stop=(c == NCH - 1))
            C.op("tensor", "matmul", psb[1][:, 0:TB], qkvw[:, w0 + 128:w0 + 256],
                 lnds[c], start=(c == 0), stop=(c == NCH - 1))
            C.op("tensor", "matmul", psb[2][:, 0:TB], qkvw[:, w0 + 256:w0 + 384],
                 lnds[c], start=(c == 0), stop=(c == NCH - 1),
                 inc=("s1mm" if c == NCH - 1 else None))
        C.wait("vector", "s1mm")
        C.op("vector", "tensor_scalar", out=sm[0][:, :], in0=psb[0][:, 0:TB],
             scalar1=colbuf[:, 35:36], scalar2=SCALE_A, op0=ALU.add, op1=ALU.mult)
        C.op("vector", "tensor_scalar", out=sm[1][:, :], in0=psb[1][:, 0:TB],
             scalar1=colbuf[:, 0:1], scalar2=None, op0=ALU.add)
        C.op("vector", "tensor_scalar", out=sm[2][:, :], in0=psb[2][:, 0:TB],
             scalar1=colbuf[:, 1:2], scalar2=None, op0=ALU.add)
        C.op("vector", "memset", med_c[:, :], 1.0)
        C.op("vector", "tensor_copy",
             bass.AP(med_c.tensor, med_c.offset, [[AR_W, 128], [2, TB]]),
             sm[2][:, :], inc="s1ev")
        C.wait("tensor", "s1ev")
        for bt in range(4):
            C.op("tensor", "transpose", psb[3][:, bt * 128:(bt + 1) * 128],
                 sm[0][:, bt * 128:(bt + 1) * 128], identt[:, :])
            C.op("tensor", "transpose", psb[4][:, bt * 128:(bt + 1) * 128],
                 sm[1][:, bt * 128:(bt + 1) * 128], identt[:, :],
                 inc=("s1tr" if bt == 3 else None))
        tdrain("s1tr")
        C.wait("vector", "s1tr")
        for bt in range(4):
            C.op("vector", "tensor_copy", med_a[:, bt * 128:(bt + 1) * 128],
                 psb[3][:, bt * 128:(bt + 1) * 128])
            C.op("vector", "tensor_reduce", med_b[:, bt:bt + 1],
                 psb[3][:, bt * 128:(bt + 1) * 128], AXX, ALU.max,
                 apply_absolute_value=True)
            C.op("vector", "tensor_reduce", med_b[:, 4 + bt:5 + bt],
                 psb[4][:, bt * 128:(bt + 1) * 128], AXX, ALU.max,
                 apply_absolute_value=True)
        C.fence("vector")
        for bt in range(4):
            C.op("vector", "tensor_tensor", out=med_b[:, 8 + bt:9 + bt],
                 in0=med_b[:, bt:bt + 1], in1=med_b[:, 4 + bt:5 + bt],
                 op=ALU.mult)
        C.fence("vector")
        C.op("vector", "tensor_scalar_mul", med_b[:, 12:16], med_b[:, 8:12],
             -1.0, inc="s1cc")
        C.wait("tensor", "s1cc")
        C.op("tensor", "transpose", psb[5][0:4, 0:128], med_b[:, 12:16],
             identt[:, :], inc="s1ct")
        tdrain("s1ct")
        C.wait("vector", "s1ct")
        C.op("vector", "tensor_copy", sm[5][0:4, 0:128], psb[5][0:4, 0:128],
             inc="s1ce")
        C.wait("gpsimd", "s1ce")
        for bt in range(4):
            C.dma("gpsimd", qst_s[bt * 128:(bt + 1) * 128, :],
                  med_a[:, bt * 128:(bt + 1) * 128], "s1qw")
            C.dma("gpsimd", crow_s[0, bt * 128:(bt + 1) * 128],
                  sm[5][bt:bt + 1, 0:128], "s1qw")
        vq = C.cnt["s1qw"]
        NB = TB // 16
        nz0 = C.cnt.get("s1nz", 0)
        st0 = C.cnt.get("s1st", 0)
        pz0 = C.cnt.get("s1pz", 0)
        for gi in range(NB):
            tt0 = gi * 16
            buf = med_d if gi % 2 == 0 else med_a
            crep = sm[3] if gi % 2 == 0 else sm[4]
            if gi < 2:
                C.wait("gpsimd", "s1qw", vq)
            if gi >= 2:
                C.wait("gpsimd", "s1nz", nz0 + (gi - 1) * 16)
            C.dma("gpsimd", buf[:, :],
                  bass.AP(qst_s.tensor, tt0 * 128, [[0, 128], [1, 2048]]),
                  "s1qb")
            if gi >= 2:
                C.wait("gpsimd", "s1st", st0 + gi - 1)
            C.dma("gpsimd", crep[:, 0:16],
                  bass.AP(crow_s.tensor, tt0, [[0, 128], [1, 16]]), "s1qb")
            C.wait("vector", "s1qb")
            C.op("vector", "tensor_tensor", out=buf[:, :], in0=buf[:, :],
                 in1=bass.AP(sm[1].tensor, sm[1].offset + gi * 16,
                             [[AR_W, 128], [1, 16], [0, 128]]),
                 op=ALU.mult)
            C.op("vector", "scalar_tensor_tensor", out=buf[:, :], in0=buf[:, :],
                 scalar=1.0,
                 in1=bass.AP(crep.tensor, crep.offset,
                             [[AR_W, 128], [1, 16], [0, 128]]),
                 op0=ALU.mult, op1=ALU.add, inc="s1st")
            C.wait("scalar", "s1st")
            C.op("scalar", "activation", out=buf[:, :], in_=buf[:, :],
                 func=AF.Exp, inc="s1ex")
            C.wait("tensor", "s1ex")
            pnz = psb[5] if gi % 2 == 0 else psb[3]
            if gi >= 2:
                C.wait("tensor", "s1pz", pz0 + gi - 1)
            for j in range(16):
                tglob = gi * 16 + j
                C.op("tensor", "matmul", pnz[:, 2 * j:2 * j + 2],
                     buf[:, j * 128:(j + 1) * 128],
                     bass.AP(med_c.tensor, med_c.offset + 2 * tglob,
                             [[AR_W, 128], [1, 2]]),
                     start=(j == 0), stop=(j == 15),
                     inc=("s1nz" if j == 15 else None),
                     incn=(16 if j == 15 else 1))
            C.wait("vector", "s1nz")
            C.op("vector", "tensor_copy", sm[5][:, 32:64], pnz[:, 0:32],
                 inc="s1pz")
            C.wait("vector", "s1pz")
            C.op("vector", "reciprocal", sm[5][:, 64:80],
                 bass.AP(sm[5].tensor, sm[5].offset + 33, [[AR_W, 128], [2, 16]]),
                 inc="s1oq")
            C.wait("vector", "s1oq")
            C.op("vector", "tensor_tensor", out=sq2[:, gi * 16:gi * 16 + 16],
                 in0=bass.AP(sm[5].tensor, sm[5].offset + 32,
                             [[AR_W, 128], [2, 16]]),
                 in1=sm[5][:, 64:80], op=ALU.mult, inc="s1oo")
        C.wait("sync", "s1oo")
        C.dma("sync", o_scr[:, t0:t0 + TB], sq2[:, :], "s1od")
        C.wait("sync", "s1od")
        rx0 = C.cnt.get("s1rx", 0)
        ra0 = C.cnt.get("s1ra", 0)
        rw0 = C.cnt.get("s1rw", 0)
        for c in range(NCH):
            a = ra_[c % 2]
            b = rb_[c % 2]
            if c >= 2:
                C.wait("sync", "s1rw", rw0 + 16 * (c - 1))
            C.dma("sync", a[:, :],
                  bass.AP(xin.tensor, c * 128 * LT + t0, [[LT, 128], [1, TB]]),
                  "s1rx")
            for r in range(4):
                C.dma("sync", b[32 * r:32 * r + 32, :],
                      bass.AP(o_scr.tensor, (4 * c + r) * LT + t0,
                              [[0, 32], [1, TB]]), "s1rx")
            C.wait("vector", "s1rx", rx0 + 80 * (c + 1))
            C.op("vector", "tensor_tensor", out=a[:, :], in0=a[:, :],
                 in1=b[:, :], op=ALU.add, inc="s1ra")
            C.wait("gpsimd", "s1ra", ra0 + c + 1)
            C.dma("gpsimd", bass.AP(x1s.tensor, c * 128 * LT + t0,
                                    [[LT, 128], [1, TB]]), a[:, :], "s1rw")
        C.wait("gpsimd", "s1rw", C.cnt["s1rw"])
        C.barrier(f"s1b{blk}")

    for blk in range(4):
        stage1_block(blk)

    def emit_ffn_block(src_t, src_tt, dst_t, dst_tt, t0, dt0):
        global self_cnt_lnmu
        self_cnt_lnmu = C.cnt.get("lnmu", 0)
        emit_ln(src_t, src_tt, t0, lnds, lnsq, lnbc)
        for c in range(NCH):
            C.op("tensor", "matmul", psb[0][0:64, 0:TB],
                 wrw[:, c * KEXP:(c + 1) * KEXP], lnds[c],
                 start=(c == 0), stop=(c == NCH - 1),
                 inc=("f2sc" if c == NCH - 1 else None))
        C.wait("vector", "f2sc")
        C.op("vector", "tensor_copy", scb[0:64, :], psb[0][0:64, 0:TB],
             inc="f2se")
        C.wait("tensor", "f2se")
        for bt in range(4):
            C.op("tensor", "transpose", psb[1][:, bt * 64:(bt + 1) * 64],
                 scb[0:64, bt * 128:(bt + 1) * 128], identt[0:64, 0:64],
                 inc=("f2tt" if bt == 3 else None))
        tdrain("f2tt")
        C.wait("vector", "f2tt")
        for bt in range(4):
            C.op("vector", "tensor_copy", stt_[:, bt * 64:(bt + 1) * 64],
                 psb[1][:, bt * 64:(bt + 1) * 64])
            C.op("vector", "tensor_copy", stw[:, bt * 64:(bt + 1) * 64],
                 psb[1][:, bt * 64:(bt + 1) * 64])
        C.fence("vector")
        for bt in range(4):
            w = stw[:, bt * 64:(bt + 1) * 64]
            o = stt_[:, bt * 64:(bt + 1) * 64]
            mcol = gcol[:, 0:1]
            m1col = gcol[:, 1:2]
            tmpc = gtmp[:, bt * 64:(bt + 1) * 64]
            for it in range(TOPK):
                C.op("vector", "tensor_reduce", mcol, w, AXX, ALU.max)
                C.fence("vector")
                if it == 0:
                    C.op("vector", "tensor_copy", m1col, mcol)
                if it < TOPK - 1:
                    C.op("vector", "tensor_scalar", out=tmpc, in0=w,
                         scalar1=mcol, scalar2=-BIG, op0=ALU.is_ge,
                         op1=ALU.mult)
                    C.op("vector", "tensor_tensor", out=w, in0=w, in1=tmpc,
                         op=ALU.add)
                    C.fence("vector")
            C.fence("vector")
            C.op("vector", "tensor_scalar", out=tmpc, in0=o,
                 scalar1=mcol, scalar2=None, op0=ALU.is_ge)
            C.op("vector", "tensor_scalar", out=o, in0=o,
                 scalar1=m1col, scalar2=None, op0=ALU.subtract, inc="f2gs")
            C.wait("scalar", "f2gs")
            C.op("scalar", "activation", out=o, in_=o, func=AF.Exp,
                 inc="f2ge")
            C.wait("vector", "f2ge")
            C.op("vector", "tensor_tensor", out=o, in0=o, in1=tmpc,
                 op=ALU.mult)
            C.fence("vector")
            C.op("vector", "tensor_reduce", mcol, o, AXX, ALU.add)
            C.fence("vector")
            C.op("vector", "reciprocal", mcol, mcol)
            C.fence("vector")
            C.op("vector", "tensor_scalar", out=o, in0=o, scalar1=mcol,
                 scalar2=FFN_SCALING, op0=ALU.mult, op1=ALU.mult,
                 inc="f2gd")
        if DBG[0]:
            C.wait("gpsimd", "f2gd", C.cnt["f2gd"])
            C.dma("gpsimd", dbg_s[0], scb[:, :], "fdbg")
            C.dma("gpsimd", dbg_s[1][:, 0:256], stt_[:, :], "fdbg")
            C.dma("gpsimd", dbg_s[2][:, 0:256], stw[:, :], "fdbg")
            C.dma("gpsimd", dbg_s[3][:, 0:4], gcol[:, :], "fdbg")
            C.dma("gpsimd", dbg_s[4], lnds[0], "fdbg")
            C.dma("gpsimd", dbg_s[5], lnds[1], "fdbg")
            for q in range(4):
                C.dma("gpsimd", dbg_s[6 + q], wrw[:, q * 512:(q + 1) * 512],
                      "fdbg")
            C.wait("gpsimd", "fdbg", C.cnt["fdbg"])
        C.wait("tensor", "f2gd")
        for bt in range(4):
            C.op("tensor", "transpose", psb[2][0:64, bt * 128:(bt + 1) * 128],
                 stt_[:, bt * 64:(bt + 1) * 64], identt[:, :],
                 inc=("f2gt" if bt == 3 else None))
        tdrain("f2gt")
        C.wait("vector", "f2gt")
        C.op("vector", "tensor_copy", scb[0:64, 0:TB], psb[2][0:64, 0:TB],
             inc="f2gc")
        C.wait("gpsimd", "f2gc")
        C.dma("gpsimd", gf_s[:, :], scb[0:64, 0:TB], "f2gw")
        C.wait("gpsimd", "f2gw")
        for rc in range(8):
            for e in range(8):
                C.dma("gpsimd",
                      gfm[16 * e:16 * e + 16, rc * TB:(rc + 1) * TB],
                      bass.AP(gf_s.tensor, (rc * 8 + e) * TB,
                              [[0, 16], [1, TB]]), "f2gb")
        C.wait("vector", "f2gb")
        aw0 = C.cnt.get("f2aw", 0)
        hm0 = C.cnt.get("f2hm", 0)
        for rc in range(8):
            wb = awb[rc % 2]
            if rc >= 2:
                C.wait("sync", "f2hm", hm0 + rc - 1)
            C.dma("sync", wb[:, :], a_t[rc], "f2aw")
            C.wait("tensor", "f2aw", aw0 + 16 * (rc + 1))
            if rc >= 2:
                C.wait("tensor", "f2hm", hm0 + rc - 1)
            pb = psb[0] if rc % 2 == 0 else psb[1]
            for c in range(NCH):
                C.op("tensor", "matmul", pb[:, 0:TB],
                     wb[:, c * 128:(c + 1) * 128], lnds[c],
                     start=(c == 0), stop=(c == NCH - 1),
                     inc=("f2hp" if c == NCH - 1 else None))
            C.wait("vector", "f2hp")
            C.op("vector", "tensor_tensor", out=hgb[:, rc * TB:(rc + 1) * TB],
                 in0=pb[:, 0:TB],
                 in1=gfm[:, rc * TB:(rc + 1) * TB], op=ALU.mult, inc="f2hm")
        bw0 = C.cnt.get("f2bw", 0)
        om0 = C.cnt.get("f2om", 0)
        ov0 = C.cnt.get("f2ov", 0)
        for co in range(NCH):
            wb = bwb[co % 2]
            if co >= 2:
                C.wait("sync", "f2om", om0 + 16 * (co - 1))
            C.dma("sync", wb[:, :], b_t[co], "f2bw")
            C.dma("sync", resb[co % 2][:, :],
                  bass.AP(src_t.tensor, co * 128 * src_tt + t0,
                          [[src_tt, 128], [1, TB]]), "f2bw")
            C.wait("tensor", "f2bw", bw0 + 32 * (co + 1))
            C.wait("tensor", "f2hm")
            pb = psb[2] if co % 2 == 0 else psb[3]
            if co >= 2:
                C.wait("tensor", "f2ov", ov0 + co - 1)
                C.wait("sync", "f2ov", ov0 + co - 1)
            for rc in range(8):
                C.op("tensor", "matmul", pb[:, 0:TB],
                     wb[:, rc * 128:(rc + 1) * 128],
                     hgb[:, rc * TB:(rc + 1) * TB],
                     start=(rc == 0), stop=(rc == 7),
                     inc=("f2op" if rc == 7 else None))
            C.wait("vector", "f2op")
            if co >= 2:
                C.wait("vector", "f2om", om0 + 16 * (co - 1))
            C.op("vector", "tensor_tensor", out=resb[co % 2][:, :],
                 in0=resb[co % 2][:, :], in1=pb[:, 0:TB], op=ALU.add,
                 inc="f2ov")
            C.wait("gpsimd", "f2ov", C.cnt["f2ov"])
            C.dma("gpsimd", bass.AP(dst_t.tensor, co * 128 * dst_tt + dt0,
                                    [[dst_tt, 128], [1, TB]]),
                  resb[co % 2][:, :], "f2om")
        C.wait("gpsimd", "f2om")

    AR.reset()
    lnb = AR.take(NCH * TB)
    lnds = [lnb[:, c * TB:(c + 1) * TB] for c in range(NCH)]
    sqa = AR.take(512)
    sqb2 = AR.take(512)
    bca = AR.take(512)
    bcb = AR.take(512)
    lnsq = (sqa, sqb2)
    lnbc = (bca, bcb)
    scb = AR.take(512)
    stt_ = AR.take(256)
    stw = AR.take(256)
    gtmp = AR.take(256)
    gcol = AR.take(4)
    gfm = AR.take(8 * TB)
    hgb = AR.take(8 * TB)
    awb = [AR.take(4096) for _ in range(2)]
    bwb = [AR.take(1024) for _ in range(2)]
    resb = [AR.take(512) for _ in range(2)]
    wrw = AR.take(NCH * KEXP)
    for c in range(NCH):
        C.dma("sync", wrw[:, c * KEXP:(c + 1) * KEXP], wr_t[c], "ldwr")
    C.wait("tensor", "ldwr", C.cnt["ldwr"])

    for blk in range(4):
        emit_ffn_block(x1s, LT, x2s, LT, blk * TB, blk * TB)
        C.barrier(f"s2b{blk}")

    if stop_after == "s2":
        with nc.Block() as block:
            C.finalize(block)
        es.close()
        return nc
    AR.reset()
    ln3h = AR.take(NCH * 1024)
    w1b = [AR.take(1024) for _ in range(2)]
    sqa3 = AR.take(512)
    sqb3 = AR.take(512)
    bca3 = AR.take(512)
    bcb3 = AR.take(512)
    stg3 = [AR.take(512) for _ in range(4)]
    xh = AR.take(LT + 3)
    cvo = AR.take(LT)

    def s3_ln_half(H):
        for b2 in range(2):
            t0 = H * 1024 + b2 * TB
            dsts = [ln3h[:, c * 1024 + b2 * TB: c * 1024 + (b2 + 1) * TB]
                    for c in range(NCH)]
            global self_cnt_lnmu
            self_cnt_lnmu = C.cnt.get("lnmu", 0)
            emit_ln(x2s, LT, t0, dsts, (sqa3, sqb3), (bca3, bcb3))

    def s3_inproj_half(H, fo_list, conv_after=None):
        wl0 = C.cnt.get("p3wl", 0)
        mm0 = C.cnt.get("p3mm", 0)
        ev0 = C.cnt.get("p3ev", 0)
        sw0 = C.cnt.get("p3sw", 0)
        nslab = 0
        nmm = 0
        nev = 0
        for fi, fo in enumerate(fo_list):
            pbs = [psb[2 * (fi % 2)], psb[2 * (fi % 2) + 1]]
            for sl in range(4):
                wb = w1b[nslab % 2]
                if nslab >= 2:
                    C.wait("sync", "p3mm", mm0 + (nslab - 1) * 8)
                C.dma("sync", wb[:, :],
                      bass.AP(w1_t.tensor, fo * 128 * 4096 + sl * 1024,
                              [[4096, 128], [1, 1024]]), "p3wl")
                nslab += 1
                C.wait("tensor", "p3wl", wl0 + 16 * nslab)
                if sl == 0 and fi >= 2:
                    C.wait("tensor", "p3ev", ev0 + 2 * (fi - 1))
                for tt in range(2):
                    pb = pbs[tt]
                    for cc in range(8):
                        c = sl * 8 + cc
                        nmm += 1
                        C.op("tensor", "matmul", pb[:, 0:TB],
                             wb[:, cc * 128:(cc + 1) * 128],
                             ln3h[:, c * 1024 + tt * TB: c * 1024 + (tt + 1) * TB],
                             start=(c == 0), stop=(c == NCH - 1),
                             inc=("p3mm" if (cc == 7) else None),
                             incn=(4 if cc == 7 else 1))
            C.wait("vector", "p3mm", mm0 + 32 * (fi + 1))
            for tt in range(2):
                sg = stg3[(2 * fi + tt) % 4]
                if 2 * fi + tt >= 4:
                    C.wait("vector", "p3sw", sw0 + 16 * (2 * fi + tt - 3))
                C.op("vector", "tensor_copy", sg[:, :], pbs[tt][:, 0:TB],
                     inc="p3ev")
                nev += 1
                C.wait("gpsimd", "p3ev", ev0 + nev)
                tglob = H * 1024 + tt * TB
                if fo < NFT_GATE:
                    dst = bass.AP(gate_s.tensor,
                                  fo * 128 * OT + tt * TB, [[OT, 128], [1, TB]])
                elif fo < NFT_GATE + NFT_XBC:
                    dst = bass.AP(xbc0_s.tensor,
                                  (fo - NFT_GATE) * 128 * LT + tglob,
                                  [[LT, 128], [1, TB]])
                else:
                    dst = bass.AP(dt_s.tensor, tglob, [[LT, 128], [1, TB]])
                C.dma("gpsimd", dst, sg[:, :], "p3sw")
            if conv_after is not None and NFT_GATE <= fo < NFT_GATE + NFT_XBC:
                conv_after(fo - NFT_GATE)

    NCV = [0]

    def emit_conv(fc):
        k = NCV[0]
        NCV[0] += 1
        C.wait("scalar", "p3sw", C.cnt["p3sw"])
        if k >= 1:
            C.wait("scalar", "c3g", C.cnt["c3g"])
        C.op("scalar", "memzero", xh[:, 0:3], inc="c3ld")
        C.wait("sync", "c3ld", C.cnt["c3ld"])
        C.dma("sync", xh[:, 3:3 + LT], xbc0_s[fc], "c3xl")
        C.wait("vector", "c3xl", C.cnt["c3xl"])
        C.op("vector", "tensor_scalar", out=xh[:, 1024:1027],
             in0=xh[:, 1024:1027], scalar1=flagc[:, 0:1], scalar2=None,
             op0=ALU.mult, inc="c3f")
        w = cwcol[:, 5 * fc:5 * fc + 4]
        b = cwcol[:, 5 * fc + 4:5 * fc + 5]
        C.wait("vector", "c3f", C.cnt["c3f"])
        if k >= 1:
            C.wait("vector", "c3w", C.cnt["c3w"])
        C.op("vector", "tensor_scalar", out=cvo[:, :], in0=xh[:, 3:3 + LT],
             scalar1=w[:, 3:4], scalar2=None, op0=ALU.mult)
        C.fence("vector")
        for j in (2, 1, 0):
            C.op("vector", "scalar_tensor_tensor", out=cvo[:, :],
                 in0=xh[:, j:j + LT], scalar=w[:, j:j + 1], in1=cvo[:, :],
                 op0=ALU.mult, op1=ALU.add)
            C.fence("vector")
        C.op("vector", "tensor_scalar", out=cvo[:, :], in0=cvo[:, :],
             scalar1=b, scalar2=None, op0=ALU.add, inc="c3g")
        C.wait("scalar", "c3g", C.cnt["c3g"])
        C.op("scalar", "activation", out=cvo[:, :], in_=cvo[:, :],
             func=AF.Silu, inc="c3s")
        C.wait("scalar", "c3s", C.cnt["c3s"])
        C.dma("scalar", xbcs_s[fc], cvo[:, :], "c3w")

    s3_ln_half(0)
    s3_inproj_half(0, list(range(NFT_GATE, NFT_IN)))
    C.barrier("s3h0")
    s3_ln_half(1)
    s3_inproj_half(1, list(range(0, NFT_IN)),
                   conv_after=emit_conv)
    C.wait("scalar", "c3w", C.cnt["c3w"])
    C.wait("gpsimd", "p3sw", C.cnt["p3sw"])
    C.barrier("s3a")

    AR.reset()
    srun = AR.take(DIN)
    xs_tm = [AR.take(DIN) for _ in range(2)]
    xsd_tm = xs_tm
    t1b = AR.take(2048)
    rrp = AR.take(2048)
    gmsb = AR.take(1024)
    bcfm = AR.take(16 * 256)
    btm = [AR.take(1024) for _ in range(2)]
    xfb = [AR.take(256) for _ in range(4)]
    dtv = AR.take(256)
    adt = AR.take(256)
    acs = AR.take(256)
    csA = AR.take(256)
    dec = AR.take(256)
    eacs = AR.take(256)
    dttm = AR.take(256)
    dectm = AR.take(256)
    acstm = AR.take(256)
    sdtm = AR.take(256)
    e_sb = AR.take(128)
    ytg = AR.take(1024)
    yst = AR.take(128)
    ccol = AR.take(4)

    def cumsum_chunk(src, dst, tmp):
        cur, oth = src, tmp
        for s in range(8):
            sh = 1 << s
            C.op("vector", "tensor_copy", oth[:, 0:sh], cur[:, 0:sh])
            C.op("vector", "tensor_tensor", out=oth[:, sh:256],
                 in0=cur[:, sh:256], in1=cur[:, 0:256 - sh], op=ALU.add)
            C.fence("vector")
            cur, oth = oth, cur
        if cur is not dst:
            C.op("vector", "tensor_copy", dst[:, :], cur[:, :])
            C.fence("vector")

    NGRP_MM = 0

    def ssd_chunk(K):
        own = K >= 4
        tk0 = K * CS
        if DBG[0]:
            C.wait("gpsimd", "t3su", C.cnt.get("t3su", 0))
            C.dma("gpsimd", srun_s[K], srun[:, :], "sdbg")
            C.wait("gpsimd", "sdbg", C.cnt["sdbg"])
        C.dma("sync", dtv[:, :], bass.AP(dt_s.tensor, tk0, [[LT, 128], [1, CS]]),
              "d3ld")
        C.wait("vector", "d3ld")
        C.op("vector", "tensor_scalar", out=dtv[:, :], in0=dtv[:, :],
             scalar1=dtcol[:, 0:1], scalar2=None, op0=ALU.add, inc="d3a")
        C.wait("scalar", "d3a", C.cnt["d3a"])
        C.op("scalar", "activation", out=csA[:, :], in_=dtv[:, :],
             func=AF.Abs, inc="d3s1")
        C.wait("vector", "d3s1", C.cnt["d3s1"])
        C.op("vector", "tensor_scalar_mul", csA[:, :], csA[:, :], -1.0,
             inc="d3s2")
        C.wait("scalar", "d3s2", C.cnt["d3s2"])
        C.op("scalar", "activation", out=csA[:, :], in_=csA[:, :],
             func=AF.Exp, inc="d3s3")
        C.wait("vector", "d3s3", C.cnt["d3s3"])
        C.op("vector", "tensor_scalar", out=csA[:, :], in0=csA[:, :],
             scalar1=1.0, scalar2=None, op0=ALU.add, inc="d3s4")
        C.wait("scalar", "d3s4", C.cnt["d3s4"])
        C.op("scalar", "activation", out=csA[:, :], in_=csA[:, :],
             func=AF.Ln)
        C.op("scalar", "activation", out=adt[:, :], in_=dtv[:, :],
             func=AF.Relu, inc="d3s5")
        C.wait("vector", "d3s5", C.cnt["d3s5"])
        C.op("vector", "tensor_tensor", out=dtv[:, :], in0=csA[:, :],
             in1=adt[:, :], op=ALU.add, inc="d3b")
        C.wait("vector", "d3b", C.cnt["d3b"])
        C.op("vector", "tensor_scalar", out=adt[:, :], in0=dtv[:, :],
             scalar1=dtcol[:, 3:4], scalar2=None, op0=ALU.mult)
        C.fence("vector")
        cumsum_chunk(adt, acs, csA)
        C.op("vector", "tensor_scalar", out=dec[:, :], in0=acs[:, :],
             scalar1=acs[:, 255:256], scalar2=-1.0, op0=ALU.subtract,
             op1=ALU.mult, inc="d3c")
        C.wait("scalar", "d3c")
        C.op("scalar", "activation", out=dec[:, :], in_=dec[:, :], func=AF.Exp)
        C.op("scalar", "activation", out=eacs[:, :], in_=acs[:, :],
             func=AF.Exp, inc="d3d")
        for e in ("vector", "tensor", "gpsimd"):
            C.wait(e, "d3d")
        C.dma("gpsimd", bass.AP(er_s.tensor, 0, [[1, 128], [0, 1]]),
              eacs[:, 255:256], "d3e")
        C.wait("gpsimd", "d3e")
        C.dma("gpsimd", e_sb[:, :],
              bass.AP(er_s.tensor, 0, [[0, 128], [1, 128]]), "d3f")
        tr0 = C.cnt.get("d3tr", 0)
        te0 = C.cnt.get("d3te", 0)
        for tb in range(2):
            if tb == 1:
                C.wait("tensor", "d3te", te0 + 1)
            C.op("tensor", "transpose", psb[6][:, 0:128],
                 dtv[:, tb * 128:(tb + 1) * 128], identt[:, :])
            C.op("tensor", "transpose", psb[6][:, 128:256],
                 dec[:, tb * 128:(tb + 1) * 128], identt[:, :])
            C.op("tensor", "transpose", psb[7][:, 0:128],
                 acs[:, tb * 128:(tb + 1) * 128], identt[:, :])
            C.op("tensor", "transpose", psb[7][:, 128:256],
                 eacs[:, tb * 128:(tb + 1) * 128], identt[:, :], inc="d3tr")
            tdrain("d3tr")
            C.wait("vector", "d3tr", tr0 + 2 * (tb + 1))
            C.op("vector", "tensor_copy", dttm[:, tb * 128:(tb + 1) * 128],
                 psb[6][:, 0:128])
            C.op("vector", "tensor_copy", dectm[:, tb * 128:(tb + 1) * 128],
                 psb[6][:, 128:256])
            C.op("vector", "tensor_copy", acstm[:, tb * 128:(tb + 1) * 128],
                 psb[7][:, 0:128])
            C.op("vector", "tensor_copy", sdtm[:, tb * 128:(tb + 1) * 128],
                 psb[7][:, 128:256], inc="d3te")
        v = C.cnt["d3te"]
        C.wait("tensor", "d3te", v)
        C.wait("gpsimd", "d3te", v)
        for g in range(16):
            C.dma("sync", bcfm[:, g * CS:(g + 1) * CS],
                  bass.AP(xbcs_s.tensor, (64 + g) * 128 * LT + tk0,
                          [[LT, 128], [1, CS]]), "d3bc")
        C.wait("tensor", "d3bc")
        bt0 = C.cnt.get("d3bt", 0)
        be0 = C.cnt.get("d3be", 0)
        for g in range(8):
            for tb in range(2):
                kk = g * 2 + tb
                if kk >= 1:
                    C.wait("tensor", "d3be", be0 + kk)
                C.op("tensor", "transpose", psb[6][:, 0:128],
                     bcfm[:, g * CS + tb * 128: g * CS + (tb + 1) * 128],
                     identt[:, :], inc="d3bt")
                tdrain("d3bt")
                C.wait("vector", "d3bt", bt0 + 2 * (kk + 1))
                C.op("vector", "tensor_copy",
                     btm[tb][:, g * 128:(g + 1) * 128], psb[6][:, 0:128],
                     inc="d3be")
        C.wait("tensor", "d3be")
        xl0 = C.cnt.get("d3xl", 0)
        xt0 = C.cnt.get("d3xt", 0)
        xe0 = C.cnt.get("d3xe", 0)
        for fc in range(64):
            xb = xfb[fc % 4]
            if fc >= 4:
                C.wait("sync", "d3xt", xt0 + 2 * (fc - 3))
            C.dma("sync", xb[:, :],
                  bass.AP(xbcs_s.tensor, fc * 128 * LT + tk0,
                          [[LT, 128], [1, CS]]), "d3xl")
            C.wait("tensor", "d3xl", xl0 + 16 * (fc + 1))
            for tb in range(2):
                pb = psb[6] if tb == 0 else psb[7]
                if 2 * fc + tb >= 2:
                    C.wait("tensor", "d3xe", xe0 + 2 * fc + tb - 1)
                C.op("tensor", "transpose", pb[:, 256:384],
                     xb[:, tb * 128:(tb + 1) * 128], identt[:, :],
                     inc="d3xt")
                C.wait("vector", "d3xt", xt0 + 2 * fc + tb + 2)
                C.op("vector", "tensor_tensor",
                     out=xs_tm[tb][:, fc * 128:(fc + 1) * 128],
                     in0=pb[:, 256:384],
                     in1=bass.AP(dttm.tensor, dttm.offset + tb * 128 + 2 * fc,
                                 [[AR_W, 128], [1, 2], [0, 64]]),
                     op=ALU.mult, inc="d3xe")
        tdrain("d3xt")
        C.wait("tensor", "d3xe", C.cnt["d3xe"])
        C.wait("vector", "d3xe", C.cnt["d3xe"])
        C.barrier(f"kx{K}")

        if own:
            ot0 = tk0 - 4 * CS
            for ib in range(2):
                for g in range(8):
                    gm0 = C.cnt.get("y3gm", 0)
                    C.wait("tensor", "y3ge", C.cnt.get("y3ge", 0))
                    for jb in range(ib + 1):
                        C.op("tensor", "matmul", psb[4][:, 0:128],
                             bcfm[:, g * CS + jb * 128: g * CS + (jb + 1) * 128],
                             bcfm[:, (8 + g) * CS + ib * 128:
                                  (8 + g) * CS + (ib + 1) * 128],
                             start=True, stop=True, inc="y3gm")
                        C.wait("vector", "y3gm", gm0 + jb + 1)
                        if jb == ib:
                            C.op("vector", "tensor_tensor",
                                 out=gmsb[:, jb * 128:(jb + 1) * 128],
                                 in0=psb[4][:, 0:128], in1=trilm[:, :],
                                 op=ALU.mult, inc="y3ge")
                        else:
                            C.op("vector", "tensor_copy",
                                 gmsb[:, jb * 128:(jb + 1) * 128],
                                 psb[4][:, 0:128], inc="y3ge")
                    C.wait("gpsimd", "y3t1", C.cnt.get("y3t1", 0))
                    C.dma("gpsimd", bass.AP(rr_s.tensor, 0, [[128, 16], [1, 128]]),
                          acs[g * 16:(g + 1) * 16, ib * 128:(ib + 1) * 128],
                          "y3rw")
                    C.wait("gpsimd", "y3rw", C.cnt["y3rw"])
                    C.dma("gpsimd", rrp[:, :],
                          bass.AP(rr_s.tensor, 0, [[0, 128], [1, 2048]]),
                          "y3rr")
                    C.wait("vector", "y3rr", C.cnt["y3rr"])
                    pA, pA2, pB, pB2 = psb[0], psb[1], psb[2], psb[3]
                    C.wait("tensor", "y3ev", C.cnt.get("y3ev", 0))
                    for hh in range(16):
                        h = g * 16 + hh
                        pp = pB if hh < 8 else pB2
                        C.op("tensor", "matmul",
                             pp[:, (hh % 8) * 64:(hh % 8 + 1) * 64],
                             bcfm[:, (8 + g) * CS + ib * 128:
                                  (8 + g) * CS + (ib + 1) * 128],
                             srun[:, h * 64:(h + 1) * 64],
                             start=True, stop=True,
                             inc=("y3ob" if hh == 15 else None))
                    for jb in range(ib + 1):
                        C.wait("vector", "y3dj", C.cnt.get("y3dj", 0))
                        C.op("vector", "tensor_tensor", out=t1b[:, :],
                             in0=rrp[:, :],
                             in1=bass.AP(acstm.tensor,
                                         acstm.offset + jb * 128 + g * 16,
                                         [[AR_W, 128], [1, 16], [0, 128]]),
                             op=ALU.subtract)
                        C.fence("vector")
                        C.op("vector", "tensor_scalar", out=t1b[:, :],
                             in0=t1b[:, :], scalar1=0.0, scalar2=None,
                             op0=ALU.min, inc="y3t1")
                        C.wait("scalar", "y3t1", C.cnt["y3t1"])
                        C.op("scalar", "activation", out=t1b[:, :],
                             in_=t1b[:, :], func=AF.Exp, inc="y3t2")
                        C.wait("vector", "y3t2", C.cnt["y3t2"])
                        C.op("vector", "tensor_tensor", out=t1b[:, :],
                             in0=t1b[:, :],
                             in1=bass.AP(gmsb.tensor, gmsb.offset + jb * 128,
                                         [[AR_W, 128], [0, 16], [1, 128]]),
                             op=ALU.mult, inc="y3t3")
                        C.wait("tensor", "y3t3", C.cnt["y3t3"])
                        for hh in range(16):
                            h = g * 16 + hh
                            pp = pA if hh < 8 else pA2
                            C.op("tensor", "matmul",
                                 pp[:, (hh % 8) * 64:(hh % 8 + 1) * 64],
                                 t1b[:, hh * 128:(hh + 1) * 128],
                                 xs_tm[jb][:, h * 64:(h + 1) * 64],
                                 start=(jb == 0), stop=(jb == ib),
                                 inc=("y3dj" if hh == 15 else None))
                    C.wait("vector", "y3dj", C.cnt["y3dj"])
                    C.wait("vector", "y3ob", C.cnt["y3ob"])
                    C.wait("vector", "y3yt", C.cnt.get("y3yt", 0))
                    for half in range(2):
                        pa = pA if half == 0 else pA2
                        pb_ = pB if half == 0 else pB2
                        C.op("vector", "tensor_tensor",
                             out=ytg[:, half * 512:(half + 1) * 512],
                             in0=pb_[:, 0:512],
                             in1=bass.AP(sdtm.tensor,
                                         sdtm.offset + ib * 128 + g * 16 + half * 8,
                                         [[AR_W, 128], [1, 8], [0, 64]]),
                             op=ALU.mult)
                        C.op("vector", "tensor_tensor",
                             out=ytg[:, half * 512:(half + 1) * 512],
                             in0=ytg[:, half * 512:(half + 1) * 512],
                             in1=pa[:, 0:512], op=ALU.add, inc="y3ev")
                    C.wait("tensor", "y3ev", C.cnt["y3ev"])
                    yt0 = C.cnt.get("y3yt", 0)
                    yw0 = C.cnt.get("y3yw", 0)
                    ye0 = C.cnt.get("y3ye", 0)
                    for fj in range(8):
                        if fj >= 1:
                            C.wait("tensor", "y3ye", ye0 + fj)
                        C.op("tensor", "transpose", psb[5][:, 0:128],
                             ytg[:, fj * 128:(fj + 1) * 128], identt[:, :],
                             inc="y3yt")
                        tdrain("y3yt")
                        C.wait("vector", "y3yt", yt0 + 2 * (fj + 1))
                        if fj >= 1:
                            C.wait("vector", "y3yw", yw0 + 16 * fj)
                        C.op("vector", "tensor_copy", yst[:, :],
                             psb[5][:, 0:128], inc="y3ye")
                        C.wait("scalar", "y3ye", C.cnt["y3ye"])
                        C.dma("scalar",
                              bass.AP(yfm_s.tensor,
                                      (g * 8 + fj) * 128 * OT + ot0 + ib * 128,
                                      [[OT, 128], [1, 128]]),
                              yst[:, :], "y3yw")

        C.wait("gpsimd", "d3xe", C.cnt["d3xe"])
        if own:
            C.wait("gpsimd", "y3dj", C.cnt.get("y3dj", 0))
            C.wait("tensor", "y3ev", C.cnt.get("y3ev", 0))
        for tb in range(2):
            C.op("gpsimd", "tensor_tensor", out=xsd_tm[tb][:, :],
                 in0=xs_tm[tb][:, :],
                 in1=bass.AP(dectm.tensor, dectm.offset + tb * 128,
                             [[AR_W, 128], [1, 128], [0, 64]]),
                 op=ALU.mult, inc="d3sd")
        v = C.cnt["d3sd"]
        C.wait("tensor", "d3sd", v)
        C.wait("vector", "d3sd", v)
        C.barrier(f"ks{K}")
        if DBG[0] and K == 0:
            C.dma("gpsimd", dbg2[0], xsd_tm[0][:, :], "sdbg")
            C.dma("gpsimd", dbg2[1], xsd_tm[1][:, :], "sdbg")
            C.dma("gpsimd", dbg2[2][:, 0:1024], btm[0][:, :], "sdbg")
            C.dma("gpsimd", dbg2[2][:, 1024:2048], btm[1][:, :], "sdbg")
            C.dma("gpsimd", dbg2[2][:, 2048:2176], e_sb[:, :], "sdbg")
            C.wait("gpsimd", "sdbg", C.cnt["sdbg"])
        sm0 = C.cnt.get("t3sm", 0)
        su0 = C.cnt.get("t3su", 0)
        C.wait("gpsimd", "d3f", C.cnt["d3f"])
        if own:
            C.wait("gpsimd", "y3ob", C.cnt.get("y3ob", 0))
        for hg in range(16):
            g = hg // 2
            pb = psb[0] if hg % 2 == 0 else psb[1]
            if hg >= 2:
                C.wait("tensor", "t3su", su0 + 2 * (hg - 1))
            for hh in range(8):
                h = hg * 8 + hh
                for tb in range(2):
                    C.op("tensor", "matmul", pb[:, hh * 64:(hh + 1) * 64],
                         btm[tb][:, g * 128:(g + 1) * 128],
                         xsd_tm[tb][:, h * 64:(h + 1) * 64],
                         start=(tb == 0), stop=(tb == 1),
                         inc=("t3sm" if (hh == 7 and tb == 1) else None),
                         incn=(16 if (hh == 7 and tb == 1) else 1))
            C.op("gpsimd", "tensor_tensor",
                 out=srun[:, hg * 512:(hg + 1) * 512],
                 in0=srun[:, hg * 512:(hg + 1) * 512],
                 in1=bass.AP(e_sb.tensor, e_sb.offset + hg * 8,
                             [[AR_W, 128], [1, 8], [0, 64]]),
                 op=ALU.mult, inc="t3sg")
            C.wait("vector", "t3sg", C.cnt["t3sg"])
            C.wait("vector", "t3sm", sm0 + 16 * (hg + 1))
            C.op("vector", "tensor_tensor",
                 out=srun[:, hg * 512:(hg + 1) * 512],
                 in0=srun[:, hg * 512:(hg + 1) * 512],
                 in1=pb[:, 0:512], op=ALU.add, inc="t3su", incn=2)
        v = C.cnt["t3su"]
        C.wait("tensor", "t3su", v)
        C.wait("gpsimd", "t3su", v)
        if K == 3:
            for hg in range(16):
                C.op("gpsimd", "tensor_scalar",
                     out=srun[:, hg * 512:(hg + 1) * 512],
                     in0=srun[:, hg * 512:(hg + 1) * 512],
                     scalar1=flagc[:, 0:1], scalar2=None, op0=ALU.mult,
                     inc="t3fl")
            v = C.cnt["t3fl"]
            C.wait("tensor", "t3fl", v)
            C.wait("vector", "t3fl", v)

    NGRP = [0]
    for hg in range(16):
        C.op("vector", "memset", srun[:, hg * 512:(hg + 1) * 512], 0.0,
             inc="s3z")
    C.wait("tensor", "s3z", C.cnt["s3z"])
    C.wait("gpsimd", "s3z", C.cnt["s3z"])
    for K in range(NCK):
        ssd_chunk(K)
        C.barrier(f"sk{K}")
    C.wait("scalar", "y3yw", C.cnt.get("y3yw", 0))
    C.barrier("s3b")

    AR.reset()
    ykb = [AR.take(OT) for _ in range(3)]
    xkb = AR.take(OT)
    gkb = AR.take(OT)
    sqk = AR.take(OT)
    rsb = AR.take(OT)
    vrow = rowbuf[0:1, 0:OT]

    def s3c_group(g):
        gy0 = C.cnt.get("n3gy", 0)
        ym0 = C.cnt.get("n3ym", 0)
        mj0 = C.cnt.get("n3mj", 0)
        for j in range(8):
            fc = g * 8 + j
            yk = ykb[j % 2]
            if j >= 2:
                C.wait("sync", "n3gy", gy0 + 16 * (j - 1))
            if j >= 1:
                C.wait("sync", "n3ym", ym0 + j)
            C.dma("sync", yk[:, :], yfm_s[fc], "n3ld")
            C.dma("sync", xkb[:, :],
                  bass.AP(xbcs_s.tensor, fc * 128 * LT + OT,
                          [[LT, 128], [1, OT]]), "n3ld")
            C.dma("sync", gkb[:, :], gate_s[fc], "n3ld")
            C.wait("scalar", "n3ld")
            C.op("scalar", "activation", out=gkb[:, :], in_=gkb[:, :],
                 func=AF.Silu, inc="n3sl")
            C.wait("vector", "n3sl")
            C.op("vector", "tensor_scalar", out=xkb[:, :], in0=xkb[:, :],
                 scalar1=dxcol[:, fc:fc + 1], scalar2=None, op0=ALU.mult)
            C.fence("vector")
            C.op("vector", "tensor_tensor", out=yk[:, :], in0=yk[:, :],
                 in1=xkb[:, :], op=ALU.add)
            C.fence("vector")
            C.op("vector", "tensor_tensor", out=yk[:, :], in0=yk[:, :],
                 in1=gkb[:, :], op=ALU.mult, inc="n3ym")
            C.wait("scalar", "n3ym")
            if j >= 1:
                C.wait("scalar", "n3mj", mj0 + j)
            C.op("scalar", "activation", out=sqk[:, :], in_=yk[:, :],
                 func=AF.Square, inc="n3sq")
            C.wait("tensor", "n3sq")
            if j == 0:
                C.wait("tensor", "n3vr", C.cnt.get("n3vr", 0))
            C.op("tensor", "matmul", psb[6][0:1, 0:512], ones_c[:, :],
                 sqk[:, 0:512], start=(j == 0), stop=(j == 7))
            C.op("tensor", "matmul", psb[7][0:1, 0:512], ones_c[:, :],
                 sqk[:, 512:1024], start=(j == 0), stop=(j == 7),
                 inc="n3mj")
            C.wait("gpsimd", "n3ym", C.cnt["n3ym"])
            C.dma("gpsimd", ygn_s[fc], yk[:, :], "n3gy")
        C.wait("vector", "n3mj", mj0 + 8)
        C.op("vector", "tensor_scalar_mul", vrow[0:1, 0:512],
             psb[6][0:1, 0:512], 1.0 / 1024.0)
        C.op("vector", "tensor_scalar_mul", vrow[0:1, 512:1024],
             psb[7][0:1, 0:512], 1.0 / 1024.0, inc="n3vr")
        C.wait("scalar", "n3vr")
        C.op("scalar", "activation", out=vrow, in_=vrow, func=AF.Sqrt,
             bias=epscol[0:1, 0:1], inc="n3rs")
        C.wait("vector", "n3rs")
        C.op("vector", "reciprocal", vrow, vrow, inc="n3rs")
        C.wait("gpsimd", "n3rs")
        C.dma("gpsimd", rrow_s[2, :], vrow[0:1, 0:512], "n3rw")
        C.dma("gpsimd", rrow_s[3, :], vrow[0:1, 512:1024], "n3rw")
        C.wait("gpsimd", "n3rw")
        C.dma("gpsimd", rsb[:, 0:512],
              rrow_s[2:3, :].to_broadcast([128, 512]), "n3rb")
        C.dma("gpsimd", rsb[:, 512:1024],
              rrow_s[3:4, :].to_broadcast([128, 512]), "n3rb")
        C.wait("vector", "n3rb", C.cnt["n3rb"])
        C.wait("sync", "n3gy", C.cnt["n3gy"])
        rn0 = C.cnt.get("n3rn", 0)
        wb0 = C.cnt.get("n3wb", 0)
        for j in range(8):
            fc = g * 8 + j
            yk = ykb[2]
            if j >= 1:
                C.wait("sync", "n3wb", wb0 + 16 * j)
            C.dma("sync", yk[:, :], ygn_s[fc], "n3r2")
            C.wait("vector", "n3r2")
            C.op("vector", "tensor_tensor", out=yk[:, :], in0=yk[:, :],
                 in1=rsb[:, :], op=ALU.mult)
            C.fence("vector")
            C.op("vector", "tensor_scalar", out=yk[:, :], in0=yk[:, :],
                 scalar1=nwcol[:, fc:fc + 1], scalar2=None, op0=ALU.mult,
                 inc="n3rn")
            C.wait("sync", "n3rn", rn0 + j + 1)
            C.dma("sync", ygn_s[fc], yk[:, :], "n3wb")
        C.wait("sync", "n3wb")

    for g in range(G):
        s3c_group(g)
    C.barrier("s3c")

    AR.reset()
    ygh = AR.take(NFT_GATE * TB)
    w2b = [AR.take(2048) for _ in range(2)]
    res2 = [AR.take(512) for _ in range(2)]
    stg4 = [AR.take(512) for _ in range(2)]

    def s3d_ttile(tt):
        C.wait("sync", "o3pw", C.cnt.get("o3pw", 0))
        for ci in range(NFT_GATE):
            C.dma("sync", ygh[:, ci * TB:(ci + 1) * TB],
                  bass.AP(ygn_s.tensor, ci * 128 * OT + tt * TB,
                          [[OT, 128], [1, TB]]), "o3yl")
        C.wait("tensor", "o3yl")
        wl0 = C.cnt.get("o3wl", 0)
        mm0 = C.cnt.get("o3mm", 0)
        ev0 = C.cnt.get("o3ev", 0)
        nslab = 0
        pw0 = C.cnt.get("o3pw", 0)
        for co in range(NCH):
            pb = psb[co % 2]
            if co >= 2:
                C.wait("tensor", "o3ev", ev0 + co - 1)
                C.wait("sync", "o3ev", ev0 + co - 1)
            C.dma("sync", res2[co % 2][:, :],
                  bass.AP(x2s.tensor, co * 128 * LT + OT + tt * TB,
                          [[LT, 128], [1, TB]]), "o3rl")
            for sl in range(4):
                wb = w2b[nslab % 2]
                if nslab >= 2:
                    C.wait("sync", "o3mm", mm0 + (nslab - 1))
                C.dma("sync", wb[:, :],
                      bass.AP(w2_t.tensor,
                              (16 * sl) * 128 * (NCH * 128) + co * 128,
                              [[NCH * 128, 128],
                               [128 * NCH * 128, 16], [1, 128]]),
                      "o3wl")
                nslab += 1
                C.wait("tensor", "o3wl", wl0 + 16 * nslab)
                for cc in range(16):
                    ci = sl * 16 + cc
                    C.op("tensor", "matmul", pb[:, 0:TB],
                         wb[:, cc * 128:(cc + 1) * 128],
                         ygh[:, ci * TB:(ci + 1) * TB],
                         start=(ci == 0), stop=(ci == NFT_GATE - 1),
                         inc=("o3mm" if cc == 15 else None))
            C.wait("vector", "o3rl", C.cnt["o3rl"])
            C.wait("vector", "o3mm", C.cnt["o3mm"])
            if co >= 2:
                C.wait("vector", "o3pw", pw0 + 16 * (co - 1))
            C.op("vector", "tensor_tensor", out=stg4[co % 2][:, :],
                 in0=res2[co % 2][:, :], in1=pb[:, 0:TB],
                 op=ALU.add, inc="o3ev")
            C.wait("gpsimd", "o3ev", C.cnt["o3ev"])
            C.dma("gpsimd", bass.AP(x3s.tensor, co * 128 * OT + tt * TB,
                                    [[OT, 128], [1, TB]]),
                  stg4[co % 2][:, :], "o3pw")
        C.wait("gpsimd", "o3pw")

    for tt in range(2):
        s3d_ttile(tt)
        C.barrier(f"s3d{tt}")

    if stop_after == "s3":
        with nc.Block() as block:
            C.finalize(block)
        es.close()
        return nc
    AR.reset()
    lnb = AR.take(NCH * TB)
    lnds = [lnb[:, c * TB:(c + 1) * TB] for c in range(NCH)]
    sqa = AR.take(512)
    sqb2 = AR.take(512)
    bca = AR.take(512)
    bcb = AR.take(512)
    lnsq = (sqa, sqb2)
    lnbc = (bca, bcb)
    scb = AR.take(512)
    stt_ = AR.take(256)
    stw = AR.take(256)
    gtmp = AR.take(256)
    gcol = AR.take(4)
    gfm = AR.take(8 * TB)
    hgb = AR.take(8 * TB)
    awb = [AR.take(4096) for _ in range(2)]
    bwb = [AR.take(1024) for _ in range(2)]
    resb = [AR.take(512) for _ in range(2)]
    wrw = AR.take(NCH * KEXP)
    for c in range(NCH):
        C.dma("sync", wrw[:, c * KEXP:(c + 1) * KEXP], wr_t[c], "ldwr")
    C.wait("tensor", "ldwr", C.cnt["ldwr"])

    for blk in range(2):
        emit_ffn_block(x3s, OT, yout, OT, blk * TB, blk * TB)
        C.barrier(f"s4b{blk}")

    with nc.Block() as block:
        C.finalize(block)
    es.close()
    return nc


def _stage_inputs(inputs):
    f = lambda k: np.asarray(inputs[k], dtype=np.float32)
    x = f("L0_attn_a_x")
    Wq, bq = f("L0_attn_a_Wq"), f("L0_attn_a_bq")
    Wk, bk = f("L0_attn_a_Wk"), f("L0_attn_a_bk")
    Wv, bv = f("L0_attn_a_Wv"), f("L0_attn_a_bv")
    W1 = f("L0_attn_b_in_proj_w")
    convw, convb = f("L0_attn_b_conv_w"), f("L0_attn_b_conv_b")
    dtb, al, Dv = f("L0_attn_b_dt_bias"), f("L0_attn_b_A_log"), f("L0_attn_b_D")
    nw, W2 = f("L0_attn_b_norm_w"), f("L0_attn_b_out_proj_w")
    A, B, Wr = f("L0_ffn_A"), f("L0_ffn_B"), f("L0_ffn_Wr")

    com = {}
    com["wq_raw"] = np.ascontiguousarray(Wq.reshape(NCH, 128, 4096))
    com["wk_t"] = np.ascontiguousarray(Wk[:, :, 0].reshape(NCH, 128, 128))
    com["wv_t"] = np.ascontiguousarray(Wv[:, :, 0].reshape(NCH, 128, 128))
    com["bq_in"] = np.ascontiguousarray(bq)
    com["bk_in"] = np.ascontiguousarray(bk)
    com["bv_in"] = np.ascontiguousarray(bv)
    com["wr_t"] = np.ascontiguousarray(Wr.T.reshape(NCH, 128, KEXP))
    A_flat = A.reshape(KEXP * RANK, DM)
    com["a_t"] = np.ascontiguousarray(
        A_flat.reshape(8, 128, NCH, 128).transpose(0, 3, 2, 1)
        .reshape(8, 128, NCH * 128))
    B_mat = B.transpose(0, 2, 1).reshape(KEXP * RANK, DM)
    com["b_t"] = np.ascontiguousarray(
        B_mat.reshape(8, 128, NCH, 128).transpose(2, 1, 0, 3)
        .reshape(NCH, 128, 8 * 128))
    com["w1_t"] = np.ascontiguousarray(
        W1.reshape(NFT_IN, 128, NCH, 128).transpose(0, 3, 2, 1)
        .reshape(NFT_IN, 128, NCH * 128))
    com["w2_t"] = np.ascontiguousarray(
        W2.reshape(NCH, 128, NFT_GATE, 128).transpose(2, 3, 0, 1)
        .reshape(NFT_GATE, 128, NCH * 128))
    com["convw"] = np.ascontiguousarray(convw[:, 0, :].reshape(NFT_XBC, 128, KCONV))
    com["convb"] = np.ascontiguousarray(convb.reshape(NFT_XBC, 128, 1))
    com["dtbias"] = np.ascontiguousarray(dtb.reshape(128, 1))
    com["alog"] = np.ascontiguousarray(al.reshape(128, 1))
    com["dexp"] = np.ascontiguousarray(Dv.repeat(64).reshape(NFT_GATE, 128, 1))
    com["normw"] = np.ascontiguousarray(nw.reshape(NFT_GATE, 128, 1))
    com["ident_in"] = np.eye(128, dtype=np.float32)
    jj = np.arange(128)[:, None]
    ii = np.arange(128)[None, :]
    com["tril_in"] = (ii >= jj).astype(np.float32)

    in_maps = []
    for core in range(NCORE):
        b, h = core // 2, core % 2
        xl = np.concatenate([x[b, (1 - h) * OT:(2 - h) * OT],
                             x[b, h * OT:(h + 1) * OT]], axis=0)
        m = dict(com)
        m["xin"] = np.ascontiguousarray(xl.T.reshape(NCH, 128, LT))
        m["flag_in"] = np.full((128, 1), float(h), dtype=np.float32)
        in_maps.append(m)
    return in_maps


def kernel(**inputs):
    in_maps = _stage_inputs(inputs)
    nc = build_program()
    res = run_bass_kernel_spmd(nc, in_maps, core_ids=list(range(NCORE)))
    out = np.empty((BATCH, SEQ, DM), dtype=np.float32)
    for core in range(NCORE):
        b, h = core // 2, core % 2
        y = np.asarray(res.results[core]["yout"])
        out[b, h * OT:(h + 1) * OT] = y.reshape(DM, OT).T
    return out
